# version 02
# baseline (speedup 1.0000x reference)
import numpy as np
import ml_dtypes
import concourse.bass as bass
import concourse.mybir as mybir
from concourse.bass_utils import run_bass_kernel_spmd

F32 = mybir.dt.float32
BF16 = mybir.dt.bfloat16
I8 = mybir.dt.int8
AF = mybir.ActivationFunctionType
ALU = mybir.AluOpType
BF = ml_dtypes.bfloat16

B, T, D = 2, 8192, 2048
H, DK, DV = 16, 128, 128
Q = 2048
QP = Q + 4
NCH = 16
INTER = 8192
LN128 = float(np.log(128.0))

ARENA_BYTES = 200 * 1024
ENGINES = ('pe', 'act', 'dve', 'pool', 'sync')


class Builder:
    def __init__(self, debug=(), stop=None):
        self.debug = debug
        self.stop_stage = stop
        self.nc = bass.Bass("TRN2", target_bir_lowering=False)
        self.ops = {k: [] for k in ENGINES}
        self.dbg_col = 0

    def op(self, eng, fn, inc=None, incn=1):
        if inc is not None:
            self.cnt[inc] += incn
            h = self.sems[inc]
            def wrapped(e, fn=fn, h=h, n=incn):
                fn(e).then_inc(h, n)
            self.ops[eng].append(wrapped)
            return self.cnt[inc]
        self.ops[eng].append(fn)
        return None

    def wait(self, eng, sem, val):
        if val is None or val <= 0:
            return
        h = self.sems[sem]
        self.ops[eng].append(lambda e, h=h, v=val: e.wait_ge(h, v))

    def c(self, sem):
        return self.cnt[sem]

    def stop(self, name):
        if self.stop_stage == name:
            self.barrier()
            return True
        return False

    def barrier(self):
        marks = [(s, self.cnt[s]) for s in ('sPE', 'sA', 'sV', 'sX', 'sQ')]
        for eng in ENGINES:
            for s, v in marks:
                self.wait(eng, s, v)

    def amark(self):
        return self.abase

    def areset(self, mark=0):
        self.abase = mark

    def buf(self, rows, cols, dtype, align=64):
        es = 2 if dtype == BF16 else 4
        a = (self.abase + align - 1) // align * align
        self.abase = a + cols * es
        assert self.abase <= ARENA_BYTES, f"arena overflow: {self.abase}"
        self.last_off = a
        return self.arena_ap.bitcast(dtype)[0:rows, a // es: a // es + cols]

    def view_at(self, off, rows, cols, dtype):
        es = 2 if dtype == BF16 else 4
        assert off % es == 0 and off + cols * es <= ARENA_BYTES
        return self.arena_ap.bitcast(dtype)[0:rows, off // es: off // es + cols]

    def dbg(self, eng, src_ap, cols, wait_list=()):
        if not self.debug:
            return
        for s, v in wait_list:
            self.wait(eng, s, v)
        o = self.dbg_col
        self.dbg_col += cols
        assert self.dbg_col <= 8192
        r = src_ap.shape[0]
        self.op(eng, lambda e, s=src_ap, o=o, r=r, w=cols: e.dma_start(
            self.d_dbg[0:r, o:o + w], s, allow_slow_non_contiguous=True),
            inc='sX', incn=16)


    def ln_stats_apply(self, src, dst, sq_sink, rows=128, var_from=None):
        op, wait, c = self.op, self.wait, self.c
        stat = self.stat
        r = slice(0, rows)
        if var_from is None:
            va = op('dve', lambda e: e.tensor_scalar(src, src, 1.0, 0.0, ALU.mult, ALU.add,
                                                     accum_out=stat[r, 0:1]), inc='sV')
            vb = op('dve', lambda e: e.scalar_tensor_tensor(sq_sink, src, 1.0, src,
                                                            ALU.mult, ALU.mult,
                                                            accum_out=stat[r, 1:2]), inc='sV')
        else:
            va = op('dve', lambda e, v=var_from: e.reduce_sum(stat[r, 0:1], v[0],
                                                              axis=mybir.AxisListType.X), inc='sV')
            vb = op('dve', lambda e, v=var_from: e.reduce_sum(stat[r, 1:2], v[1],
                                                              axis=mybir.AxisListType.X), inc='sV')
        wait('dve', 'sV', vb)
        op('dve', lambda e: e.tensor_scalar(stat[r, 2:3], stat[r, 0:1], 1.0 / D, None, ALU.mult))
        vc = op('dve', lambda e: e.tensor_scalar(stat[r, 3:4], stat[r, 1:2], 1.0 / D, None,
                                                 ALU.mult), inc='sV')
        wait('dve', 'sV', vc)
        vd = op('dve', lambda e: e.tensor_tensor(stat[r, 4:5], stat[r, 2:3], stat[r, 2:3],
                                                 ALU.mult), inc='sV')
        wait('dve', 'sV', vd)
        v1 = op('dve', lambda e: e.tensor_tensor(stat[r, 5:6], stat[r, 3:4], stat[r, 4:5],
                                                 ALU.subtract), inc='sV')
        wait('act', 'sV', v1)
        op('act', lambda e: e.activation(stat[r, 6:7], stat[r, 5:6], AF.Ln,
                                         bias=self.cst['fc'].ap()[r, 0:1]))
        a2 = op('act', lambda e: e.activation(stat[r, 6:7], stat[r, 6:7], AF.Exp, scale=-0.5),
                inc='sA')
        wait('dve', 'sA', a2)
        ve = op('dve', lambda e: e.tensor_scalar(stat[r, 7:8], stat[r, 2:3], stat[r, 6:7],
                                                 None, ALU.mult), inc='sV')
        wait('dve', 'sV', ve)
        v2 = op('dve', lambda e: e.tensor_scalar(dst, src, stat[r, 6:7], stat[r, 7:8],
                                                 ALU.mult, ALU.subtract), inc='sV')
        return v2

    def tr_batch(self, pairs, pe_wait=None, dst_wait=None):
        assert len(pairs) <= 8
        b = self.pt_next
        self.pt_next = 1 - self.pt_next
        PT = self.PTb[b]
        self.wait('pe', 'sA', self.pt_free[b])
        if pe_wait:
            self.wait('pe', *pe_wait)
        pc = None
        for i, (src, dst) in enumerate(pairs):
            p, f = src.shape
            ident = self.cst['ident'].ap()[0:p, 0:p]
            pc = self.op('pe', lambda e, i=i, src=src, p=p, f=f, ident=ident, PT=PT:
                         e.transpose(PT[0:f, i * 128:i * 128 + p], src, ident),
                         inc='sPE' if i == len(pairs) - 1 else None)
        self.wait('act', 'sPE', pc)
        if dst_wait:
            self.wait('act', *dst_wait)
        ac = None
        for i, (src, dst) in enumerate(pairs):
            p, f = src.shape
            ac = self.op('act', lambda e, i=i, dst=dst, p=p, f=f, PT=PT: e.copy(
                dst, PT[0:f, i * 128:i * 128 + p]), inc='sA')
        self.pt_free[b] = ac
        return ac

    def tr(self, src_bf, dst_bf, pe_wait=None, dst_wait=None):
        return self.tr_batch([(src_bf, dst_bf)], pe_wait=pe_wait, dst_wait=dst_wait)

    def build(self):
        nc = self.nc
        dt = nc.dram_tensor
        self.xt = dt("xt", [QP, D], F32, kind="ExternalInput").ap()
        self.wqkv = dt("wqkv", [D, 3 * D], F32, kind="ExternalInput").ap()
        self.convw = dt("convw", [H * 3 * 128, 4], F32, kind="ExternalInput").ap()
        self.wz = dt("wz", [D, D], F32, kind="ExternalInput").ap()
        self.wba = dt("wba", [D, 64], F32, kind="ExternalInput").ap()
        self.dtab = dt("dtab", [64, 2], F32, kind="ExternalInput").ap()
        self.nw = dt("nw", [1, 128], F32, kind="ExternalInput").ap()
        self.wout = dt("wout", [D, D], F32, kind="ExternalInput").ap()
        self.wg8 = dt("wg8", [D, INTER], I8, kind="ExternalInput").ap()
        self.wu8 = dt("wu8", [D, INTER], I8, kind="ExternalInput").ap()
        self.wd8 = dt("wd8", [INTER, D], I8, kind="ExternalInput").ap()
        self.sgt = dt("sgt", [128, 64], F32, kind="ExternalInput").ap()
        self.sut = dt("sut", [128, 64], F32, kind="ExternalInput").ap()
        self.sdr = dt("sdr", [1, D], F32, kind="ExternalInput").ap()
        self.sel = dt("sel", [128, 4], F32, kind="ExternalInput").ap()
        self.i_ident = dt("ident", [128, 128], BF16, kind="ExternalInput").ap()
        self.i_identf = dt("identf", [128, 128], F32, kind="ExternalInput").ap()
        self.i_onesb = dt("onesb", [1, 128], BF16, kind="ExternalInput").ap()
        self.i_onesf = dt("onesf", [1, 128], F32, kind="ExternalInput").ap()
        self.i_onescol = dt("onescol", [128, 1], BF16, kind="ExternalInput").ap()
        self.i_ones16 = dt("ones16", [16, 128], F32, kind="ExternalInput").ap()
        self.i_msl = dt("m_sl", [128, 128], I8, kind="ExternalInput").ap()
        self.i_msu = dt("m_su", [128, 128], I8, kind="ExternalInput").ap()
        self.i_miu = dt("m_iu", [128, 128], I8, kind="ExternalInput").ap()
        self.i_fc = dt("fconst", [128, 4], F32, kind="ExternalInput").ap()

        self.y = dt("y", [Q, D], F32, kind="ExternalOutput").ap()

        self.d_qkv = dt("d_qkv", [H, 3, 128, QP], BF16).ap()
        self.d_z = dt("d_z", [H, 128, Q], BF16).ap()
        self.d_blk = dt("d_blk", [H, NCH, 128, 512], BF16).ap()
        self.d_maps = dt("d_maps", [128, H * 256], BF16).ap()
        self.d_gath = dt("d_gath", [4 * 128, H * 256], BF16).ap()
        self.d_onrm = dt("d_onrm", [H, 128, Q], BF16).ap()
        self.d_x2 = dt("d_x2", [NCH, 128, D], F32).ap()
        self.d_fuse = dt("d_fuse", [64, 128, Q], BF16).ap()
        self.d_x4p = dt("d_x4p", [4, NCH, 128, 512], BF16).ap()
        if self.debug:
            self.d_dbg = dt("d_dbg", [128, 8192], F32, kind="ExternalOutput").ap()

        from contextlib import ExitStack
        es = ExitStack()
        with es:
            arena = es.enter_context(nc.sbuf_tensor("arena", [128, ARENA_BYTES // 4], F32))
            c_ident = es.enter_context(nc.sbuf_tensor("c_ident", [128, 128], BF16))
            c_identf = es.enter_context(nc.sbuf_tensor("c_identf", [128, 128], F32))
            c_onesb = es.enter_context(nc.sbuf_tensor("c_onesb", [1, 128], BF16))
            c_onesf = es.enter_context(nc.sbuf_tensor("c_onesf", [1, 128], F32))
            c_onescol = es.enter_context(nc.sbuf_tensor("c_onescol", [128, 1], BF16))
            c_ones16 = es.enter_context(nc.sbuf_tensor("c_ones16", [16, 128], F32))
            c_msl = es.enter_context(nc.sbuf_tensor("c_msl", [128, 128], I8))
            c_msu = es.enter_context(nc.sbuf_tensor("c_msu", [128, 128], I8))
            c_miu = es.enter_context(nc.sbuf_tensor("c_miu", [128, 128], I8))
            c_zero = es.enter_context(nc.sbuf_tensor("c_zero", [128, 128], BF16))
            c_nwrow = es.enter_context(nc.sbuf_tensor("c_nwrow", [1, 128], F32))
            c_nwbc = es.enter_context(nc.sbuf_tensor("c_nwbc", [128, 128], F32))
            c_sel = es.enter_context(nc.sbuf_tensor("c_sel", [128, 4], F32))
            c_sgt = es.enter_context(nc.sbuf_tensor("c_sgt", [128, 64], F32))
            c_sut = es.enter_context(nc.sbuf_tensor("c_sut", [128, 64], F32))
            c_dtab = es.enter_context(nc.sbuf_tensor("c_dtab", [64, 2], F32))
            c_fc = es.enter_context(nc.sbuf_tensor("c_fc", [128, 4], F32))
            stat = es.enter_context(nc.sbuf_tensor("stat", [128, 48], F32))
            taps = es.enter_context(nc.sbuf_tensor("taps", [128, 24], F32))
            p_rho = es.enter_context(nc.sbuf_tensor("p_rho", [128, NCH * 16], F32))
            P0 = es.enter_context(nc.psum_tensor("P0", [128, 512], F32))
            P1 = es.enter_context(nc.psum_tensor("P1", [128, 512], F32))
            P3 = es.enter_context(nc.psum_tensor("P3", [128, 512], F32))
            P4 = es.enter_context(nc.psum_tensor("P4", [128, 512], F32))
            P5 = es.enter_context(nc.psum_tensor("P5", [128, 512], F32))
            P6 = es.enter_context(nc.psum_tensor("P6", [128, 512], F32))
            PTB = es.enter_context(nc.psum_tensor("PTB", [128, 1024], BF16))
            PTB2 = es.enter_context(nc.psum_tensor("PTB2", [128, 1024], BF16))
            hQ = es.enter_context(nc.semaphore("hQ"))
            hX = es.enter_context(nc.semaphore("hX"))
            hPE = es.enter_context(nc.semaphore("hPE"))
            hA = es.enter_context(nc.semaphore("hA"))
            hV = es.enter_context(nc.semaphore("hV"))
            hC = es.enter_context(nc.semaphore("hC"))
            self.arena_ap = arena.ap()
            self.sems = {'sQ': hQ, 'sX': hX, 'sPE': hPE, 'sA': hA, 'sV': hV, 'sC': hC}
            self.cnt = {k: 0 for k in self.sems}
            self.P = [P0, P1, None, P3, P4, P5, P6]
            self.PTb = [PTB.ap(), PTB2.ap()]
            self.cst = dict(ident=c_ident, identf=c_identf, onesb=c_onesb,
                            onesf=c_onesf, onescol=c_onescol, ones16=c_ones16,
                            msl=c_msl, msu=c_msu, miu=c_miu, zero=c_zero,
                            nwrow=c_nwrow, nwbc=c_nwbc, sel=c_sel, sgt=c_sgt,
                            sut=c_sut, dtab=c_dtab, fc=c_fc)
            self.stat = stat.ap()
            self.taps = taps.ap()
            self.p_rho = p_rho.ap()
            self.pt_free = [0, 0]
            self.pt_next = 0
            self.abase = 0

            with nc.Block() as block:
                self.emit_all()

                @block.sync
                def _(e):
                    for f in self.ops['sync']:
                        f(e)

                @block.tensor
                def _(e):
                    for f in self.ops['pe']:
                        f(e)

                @block.scalar
                def _(e):
                    for f in self.ops['act']:
                        f(e)

                @block.vector
                def _(e):
                    for f in self.ops['dve']:
                        f(e)

                @block.gpsimd
                def _(e):
                    for f in self.ops['pool']:
                        f(e)

        return nc

    def emit_all(self):
        op, wait, c = self.op, self.wait, self.c
        cst, stat, P = self.cst, self.stat, self.P

        for name, src in [('ident', self.i_ident), ('identf', self.i_identf),
                          ('onesb', self.i_onesb), ('onesf', self.i_onesf),
                          ('onescol', self.i_onescol), ('ones16', self.i_ones16),
                          ('msl', self.i_msl), ('msu', self.i_msu), ('miu', self.i_miu),
                          ('nwrow', self.nw), ('sel', self.sel),
                          ('sgt', self.sgt), ('sut', self.sut), ('dtab', self.dtab),
                          ('fc', self.i_fc)]:
            op('sync', lambda e, d=cst[name].ap(), s=src: e.dma_start(d, s), inc='sX', incn=16)
        x_const = c('sX')
        op('dve', lambda e: e.memset(cst['zero'].ap(), 0.0), inc='sV')
        wait('pe', 'sX', x_const)
        p_nw = op('pe', lambda e: e.matmul(P[6][:, 0:128], cst['onesf'].ap(), cst['nwrow'].ap(),
                                           start=True, stop=True), inc='sPE')
        wait('dve', 'sPE', p_nw)
        op('dve', lambda e: e.tensor_copy(cst['nwbc'].ap(), P[6][:, 0:128]), inc='sV')

        if self.stop('consts'):
            return
        self.areset(0)
        xnf = self.buf(128, 16 * QP, BF16)
        qkvA = self.buf(128, 2 * 3 * QP, BF16)
        pan_lq = self.buf(16, Q, F32)
        off_panlq = self.last_off
        pan_lk = self.buf(16, Q, F32)
        off_panlk = self.last_off
        pan_lnb = self.buf(16, Q, F32)
        pan_gam = self.buf(16, Q, F32)
        mark_passA = self.amark()
        whead = self.buf(128, 2 * 64 * 128, BF16)
        ztmp = self.buf(128, Q, BF16)
        ztok = self.buf(128, Q, BF16)
        junk = self.buf(128, Q, BF16)
        rqrow = self.buf(1, Q, F32)
        mark_mixed = self.amark()
        mixed = self.buf(128, QP, F32)
        if self.debug:
            print('offs: xnf=0?, off_panlq=%d off_panlk=%d mark_passA=%d mark_mixed=%d abase=%d'
                  % (off_panlq, off_panlk, mark_passA, mark_mixed, self.abase))
        convacc = self.buf(128, Q, F32)
        self.areset(mark_mixed)
        xtile = self.buf(128, 2 * D, BF16)
        self.areset(mark_mixed)
        pan_bg = self.buf(64, QP, F32)[:, 0:Q]
        pan_sp = self.buf(16, QP, F32)[:, 0:Q]

        ln1_free = [0, 0]
        ln1_v_done = []
        for i in range(16):
            wait('pool', 'sPE', ln1_free[i % 2])
            op('pool', lambda e, i=i: e.dma_start(
                xtile[:, (i % 2) * D:(i % 2) * D + D],
                self.xt[4 + i * 128:4 + (i + 1) * 128, :]), inc='sQ', incn=16)
            q_ld = c('sQ')
            xti = xtile[:, (i % 2) * D:(i % 2) * D + D]
            wait('dve', 'sQ', q_ld)
            wait('dve', 'sPE', ln1_free[i % 2])
            v2 = self.ln_stats_apply(xti, xti, junk[:, 0:D])
            ln1_v_done.append(v2)
            for g in range(2):
                self.tr_batch([(xti[:, ft * 128:(ft + 1) * 128],
                                xnf[:, ft * QP + 4 + i * 128:ft * QP + 4 + (i + 1) * 128])
                               for ft in range(g * 8, g * 8 + 8)],
                              pe_wait=('sV', v2) if g == 0 else None)
            ln1_free[i % 2] = c('sPE')
        wait('pool', 'sPE', ln1_free[0])
        op('pool', lambda e: e.dma_start(xtile[:, 0:D], self.xt[0:128, :]), inc='sQ', incn=16)
        q_ld = c('sQ')
        xti = xtile[:, 0:D]
        wait('dve', 'sQ', q_ld)
        wait('dve', 'sPE', ln1_free[0])
        v2 = self.ln_stats_apply(xti, xti, junk[:, 0:D])
        if 'pref2' in self.debug:
            self.dbg('sync', xtile[:, 0:128].bitcast(F32), 64, [('sV', v2)])
        for g in range(2):
            b = self.pt_next
            self.pt_next = 1 - self.pt_next
            PTx = self.PTb[b]
            self.wait('pe', 'sA', self.pt_free[b])
            self.wait('pe', 'sV', v2)
            pc = None
            for j, ft in enumerate(range(g * 8, g * 8 + 8)):
                pc = op('pe', lambda e, j=j, ft=ft, PTx=PTx: e.transpose(
                    PTx[:, j * 128:(j + 1) * 128], xtile[:, ft * 128:(ft + 1) * 128],
                    cst['ident'].ap()), inc='sPE' if j == 7 else None)
            wait('act', 'sPE', pc)
            ac = None
            for j, ft in enumerate(range(g * 8, g * 8 + 8)):
                ac = op('act', lambda e, j=j, ft=ft, PTx=PTx: e.copy(
                    xnf[:, ft * QP:ft * QP + 4], PTx[:, j * 128:j * 128 + 4]), inc='sA')
            self.pt_free[b] = ac
        a_xnf = c('sA')
        if 'pref2' in self.debug:
            self.dbg('sync', xnf[:, 0:4].bitcast(F32), 2, [('sA', a_xnf)])
        self.dbg('sync', xnf[:, 0:1024].bitcast(F32), 512, [('sA', a_xnf)]) if 'xnf' in self.debug else None

        if self.stop('ln1'):
            return
        op('pool', lambda e: e.dma_start(
            whead[:, 0:16 * 64].rearrange("p (kt m) -> p kt m", kt=16),
            self.wba.rearrange("(kt p) m -> p kt m", p=128)), inc='sQ', incn=16)
        q_wba = c('sQ')
        wait('pe', 'sQ', q_wba)
        wait('pe', 'sA', a_xnf)
        a_bg = 0
        for nt in range(4):
            pb = P[nt % 2]
            wait('pe', 'sA', a_bg)
            pc = None
            for kt in range(16):
                pc = op('pe', lambda e, nt=nt, kt=kt, pb=pb: e.matmul(
                    pb[0:64, 0:512], whead[:, kt * 64:(kt + 1) * 64],
                    xnf[:, kt * QP + 4 + nt * 512:kt * QP + 4 + (nt + 1) * 512],
                    start=(kt == 0), stop=(kt == 15)), inc='sPE' if kt == 15 else None)
            wait('act', 'sPE', pc)
            a_bg = op('act', lambda e, nt=nt, pb=pb: e.copy(pan_bg[:, nt * 512:(nt + 1) * 512],
                                                            pb[0:64, 0:512]), inc='sA')
        pe_bg_done = c('sPE')
        wait('act', 'sA', a_bg)
        op('act', lambda e: e.activation(pan_sp, pan_bg[0:16, :], AF.Abs))
        op('act', lambda e: e.activation(pan_sp, pan_sp, AF.Exp, scale=-1.0))
        a3 = op('act', lambda e: e.activation(pan_sp, pan_sp, AF.Ln, bias=cst['fc'].ap()[0:16, 2:3]), inc='sA')
        wait('dve', 'sA', a3)
        op('dve', lambda e: e.tensor_scalar(pan_lnb, pan_bg[0:16, :], 0.0, None, ALU.min))
        v_lnb = op('dve', lambda e: e.tensor_tensor(pan_lnb, pan_lnb, pan_sp, ALU.subtract),
                   inc='sV')
        wait('dve', 'sA', a_bg)
        v_dt = op('dve', lambda e: e.tensor_scalar(pan_bg[32:48, :], pan_bg[32:48, :],
                                                   cst['dtab'].ap()[32:48, 0:1], None, ALU.add),
                  inc='sV')
        wait('act', 'sV', v_dt)
        op('act', lambda e: e.activation(pan_sp, pan_bg[32:48, :], AF.Abs))
        op('act', lambda e: e.activation(pan_sp, pan_sp, AF.Exp, scale=-1.0))
        op('act', lambda e: e.activation(pan_sp, pan_sp, AF.Ln, bias=cst['fc'].ap()[0:16, 2:3]))
        a_g = op('act', lambda e: e.activation(stat[32:48, 8:9], cst['dtab'].ap()[32:48, 1:2],
                                               AF.Exp), inc='sA')
        wait('dve', 'sA', a_g)
        op('dve', lambda e: e.tensor_scalar(pan_gam, pan_bg[32:48, :], 0.0, None, ALU.max))
        op('dve', lambda e: e.tensor_tensor(pan_sp, pan_gam, pan_sp, ALU.add))
        v_g = op('dve', lambda e: e.tensor_scalar(pan_sp, pan_sp, stat[32:48, 8:9], -1.0,
                                                  ALU.mult, ALU.mult), inc='sV')
        wait('dve', 'sV', v_g)
        for ch in range(NCH):
            op('dve', lambda e, ch=ch: e.tensor_tensor_scan(
                pan_gam[:, ch * 128:(ch + 1) * 128], cst['ones16'].ap(),
                pan_sp[:, ch * 128:(ch + 1) * 128], 0.0, ALU.mult, ALU.add))
        v_gamma = op('dve', lambda e: e.engine_nop(), inc='sV')
        if 'gb' in self.debug:
            self.dbg('sync', pan_gam[:, 0:512], 512, [('sV', v_gamma)])
            self.dbg('sync', pan_lnb[:, 0:512], 512, [('sV', v_lnb)])

        if 'pans0' in self.debug:
            self.dbg('sync', pan_gam[:, 0:512], 512, [('sV', v_gamma), ('sA', c('sA')), ('sV', c('sV'))])
            self.dbg('sync', pan_lnb[:, 0:512], 512)
        if self.stop('bg'):
            return
        wait('act', 'sV', v_gamma)
        wait('dve', 'sV', v_gamma)
        headA_pe = []
        headA_spill = []
        a_pref_prev = 0
        z_spill_prev = 0
        row_dma_prev = 0
        pe_junk_prev = 0
        for h in range(H):
            if 'nodma' in self.debug:
                headA_spill.append(c('sX'))
                headA_pe.append(c('sPE'))
                continue
            ws = (h % 2) * 64 * 128
            qs = (h % 2) * 3 * QP
            if h >= 2:
                wait('pool', 'sPE', headA_pe[h - 2])
                wait('pool', 'sX', headA_spill[h - 2])
            else:
                wait('pool', 'sPE', pe_bg_done)
            for gi, (src, col) in enumerate([(self.wqkv, h * 128), (self.wqkv, D + h * 128),
                                             (self.wqkv, 2 * D + h * 128), (self.wz, h * 128)]):
                op('pool', lambda e, s=src, cl=col, ws=ws, gi=gi: e.dma_start(
                    whead[:, ws + gi * 2048:ws + (gi + 1) * 2048].rearrange(
                        "p (kt m) -> p kt m", kt=16),
                    s[:, cl:cl + 128].rearrange("(kt p) m -> p kt m", p=128)), inc='sQ', incn=16)
            tb = (h % 2) * 12
            if 'notaps' not in self.debug:
                op('pool', lambda e, h=h, tb=tb: e.dma_start(
                    self.taps[:, tb:tb + 12].rearrange("p (m t) -> p m t", m=3),
                    self.convw[h * 384:(h + 1) * 384, :].rearrange("(m p) t -> p m t", p=128)),
                    inc='sQ', incn=16)

            q_w = c('sQ')
            if 'dmaonly' in self.debug:
                headA_spill.append(c('sX'))
                headA_pe.append(c('sPE'))
                continue
            wait('pe', 'sQ', q_w)
            if h == 0:
                wait('pe', 'sA', a_xnf)
            wait('pe', 'sA', a_pref_prev)
            pc = None
            for m in range(3):
                for kt in range(16):
                    pc = op('pe', lambda e, m=m, kt=kt, ws=ws: e.matmul(
                        P[4][:, m * 4:(m + 1) * 4],
                        whead[:, ws + m * 2048 + kt * 128:ws + m * 2048 + (kt + 1) * 128],
                        xnf[:, kt * QP:kt * QP + 4],
                        start=(kt == 0), stop=(kt == 15)), inc='sPE' if kt == 15 else None)
            wait('act', 'sPE', pc)
            a_pref = op('act', lambda e: e.copy(stat[:, 12:24], P[4][:, 0:12]), inc='sA')
            a_pref_prev = a_pref
            if 'prefd' in self.debug and h == 0:
                self.dbg('sync', stat[:, 12:24], 12, [('sA', a_pref)])
                self.dbg('sync', xnf[:, 0:4].bitcast(F32), 2)
                self.dbg('sync', xnf[:, QP:QP + 4].bitcast(F32), 2)

            bank_free = {0: 0, 1: 0, 3: 0}
            conv_done = [0, 0, 0]
            a_z = 0
            for m in range(4):
                for nt in range(4):
                    bk = [0, 1, 3][(m * 4 + nt) % 3]
                    pb = P[bk]
                    wait('pe', 'sA', bank_free[bk])
                    pc = None
                    for kt in range(16):
                        pc = op('pe', lambda e, m=m, kt=kt, nt=nt, ws=ws, pb=pb: e.matmul(
                            pb[:, 0:512],
                            whead[:, ws + m * 2048 + kt * 128:ws + m * 2048 + (kt + 1) * 128],
                            xnf[:, kt * QP + 4 + nt * 512:kt * QP + 4 + (nt + 1) * 512],
                            start=(kt == 0), stop=(kt == 15)), inc='sPE' if kt == 15 else None)
                    wait('act', 'sPE', pc)
                    if m < 3:
                        if nt == 0 and m >= 1:
                            wait('act', 'sV', conv_done[m - 1])
                        ac = op('act', lambda e, nt=nt, pb=pb: e.copy(
                            mixed[:, 4 + nt * 512:4 + (nt + 1) * 512], pb[:, 0:512]), inc='sA')
                    else:
                        ac = op('act', lambda e, nt=nt, pb=pb: e.activation(
                            ztmp[:, nt * 512:(nt + 1) * 512], pb[:, 0:512], AF.Silu), inc='sA')
                        a_z = ac
                    bank_free[bk] = ac
                if m < 3:
                    qb = qs + m * QP
                    wait('dve', 'sA', ac)
                    wait('dve', 'sA', a_pref)
                    op('dve', lambda e, m=m: e.tensor_copy(mixed[:, 0:4],
                                                           stat[:, 12 + m * 4:12 + (m + 1) * 4]))
                    tapc = lambda i, m=m, tb=tb: self.taps[:, tb + m * 4 + i:tb + m * 4 + i + 1]
                    wait('dve', 'sQ', q_w)
                    op('dve', lambda e, t=tapc: e.tensor_scalar(
                        convacc, mixed[:, 1:1 + Q], t(0), None, ALU.mult))
                    for i in (1, 2):
                        op('dve', lambda e, i=i, t=tapc: e.scalar_tensor_tensor(
                            convacc, mixed[:, 1 + i:1 + i + Q], t(i), convacc, ALU.mult, ALU.add))
                    vc = op('dve', lambda e, qb=qb, t=tapc: e.scalar_tensor_tensor(
                        qkvA[:, qb + 4:qb + 4 + Q], mixed[:, 4:4 + Q], t(3), convacc,
                        ALU.mult, ALU.add), inc='sV')
                    conv_done[m] = vc
                    wait('act', 'sV', vc)
                    a_si = op('act', lambda e, qb=qb: e.activation(
                        qkvA[:, qb + 4:qb + 4 + Q], qkvA[:, qb + 4:qb + 4 + Q], AF.Silu), inc='sA')
                    if 'mixd' in self.debug and h == 0 and m == 0:
                        self.dbg('sync', mixed[:, 0:64], 64, [('sV', vc)])
                        self.dbg('sync', qkvA[:, qb:qb + 128].bitcast(F32), 64,
                                 [('sA', a_si)])
                    if m < 2 and 'norq' not in self.debug:
                        wait('dve', 'sA', a_si)
                        wait('dve', 'sPE', pe_junk_prev)
                        wait('dve', 'sX', z_spill_prev)
                        vq = op('dve', lambda e, qb=qb: e.tensor_tensor(
                            junk[:, 0:Q], qkvA[:, qb + 4:qb + 4 + Q], qkvA[:, qb + 4:qb + 4 + Q],
                            ALU.mult), inc='sV')
                        wait('pe', 'sV', vq)
                        a_ev = getattr(self, 'rq_ev', 0)
                        for nt in range(4):
                            wait('pe', 'sA', a_ev)
                            pc2 = op('pe', lambda e, nt=nt: e.matmul(
                                P[5][0:1, 0:512], cst['onescol'].ap(),
                                junk[:, nt * 512:(nt + 1) * 512], start=True, stop=True),
                                inc='sPE')
                            wait('act', 'sPE', pc2)
                            wait('act', 'sX', row_dma_prev)
                            a_ev = op('act', lambda e, nt=nt: e.copy(
                                rqrow[0:1, nt * 512:(nt + 1) * 512], P[5][0:1, 0:512]), inc='sA')
                        pe_junk_prev = pc2
                        self.rq_ev = a_ev
                        wait('sync', 'sA', a_ev)
                        dst = pan_lq if m == 0 else pan_lk
                        row_dma_prev = op('sync', lambda e, d=dst, h=h: e.dma_start(
                            d[h:h + 1, :], rqrow[0:1, :]), inc='sX', incn=16)
                        if 'rqd' in self.debug and h == 0 and m == 0:
                            self.dbg('sync', junk[:, 0:128].bitcast(F32), 64,
                                     [('sV', vq), ('sX', row_dma_prev)])
                            self.dbg('sync', rqrow[0:1, 0:512], 512)
            wait('act', 'sPE', pe_junk_prev)
            for g in range(2):
                self.tr_batch([(ztmp[:, ch * 128:(ch + 1) * 128], ztok[:, ch * 128:(ch + 1) * 128])
                               for ch in range(g * 8, g * 8 + 8)],
                              pe_wait=('sA', a_z), dst_wait=('sX', z_spill_prev))
            a_zt = c('sA')
            wait('sync', 'sA', a_zt)
            z_spill_prev = op('sync', lambda e, h=h: e.dma_start(self.d_z[h], ztok),
                              inc='sX', incn=16)
            wait('sync', 'sV', conv_done[2])
            wait('sync', 'sA', c('sA'))
            sp = op('sync', lambda e, h=h, qs=qs: e.dma_start(
                self.d_qkv[h].rearrange("m p t -> p m t"),
                qkvA[:, qs:qs + 3 * QP].rearrange("p (m t) -> p m t", m=3)), inc='sX', incn=16)
            headA_spill.append(sp)
            headA_pe.append(c('sPE'))
            if 'pansH' in self.debug and h in (0, 3, 7, 15):
                self.dbg('sync', pan_lq[0:1, 0:512], 512, [('sX', c('sX'))])
        if 'qkvd' in self.debug:
            self.dbg('sync', qkvA[:, 3 * QP + 4:3 * QP + 4 + 256].bitcast(F32), 128,
                     [('sX', c('sX'))])
            self.dbg('sync', qkvA[:, 3 * QP + QP + 4:3 * QP + QP + 4 + 256].bitcast(F32), 128)
        x_passA = c('sX')

        if 'pansA' in self.debug:
            self.dbg('sync', pan_gam[:, 0:512], 512, [('sX', x_passA), ('sV', v_gamma)])
            self.dbg('sync', pan_lq[:, 0:512], 512)
        if self.stop('passA'):
            return
        comb1 = self.view_at(off_panlq, 128, Q, F32)
        colp1 = self.view_at(off_panlk, 128, NCH * 128, F32)
        self.areset(mark_passA)
        p_phi = self.buf(128, NCH * 16, F32)
        p_eta = self.buf(128, NCH * 16, F32)
        p_lam = self.buf(128, NCH * 16, F32)
        p_bet = self.buf(128, NCH * 16, F32)
        gcbc = self.buf(128, NCH * 16, F32)
        etmp = self.buf(128, NCH * 16, F32)
        gcrow = self.buf(1, NCH * 16, F32)
        rowst = self.buf(1, 3 * Q, F32)
        dwork = self.buf(128, 48 * 128, BF16)
        xapp = self.buf(128, 4 * 256, BF16)
        blkst = self.buf(128, NCH * 512, BF16)
        mapst = self.buf(128, H * 256, BF16)
        compw = self.buf(128, 4 * 128, BF16)

        self.barrier()
        if 'pans' in self.debug:
            self.dbg('sync', pan_lq[:, 0:512], 512, [('sX', x_passA)])
            self.dbg('sync', pan_lk[:, 0:512], 512)
            self.dbg('sync', pan_gam[:, 0:512], 512)
            self.dbg('sync', pan_lnb[:, 0:512], 512)
        wait('act', 'sX', x_passA)
        op('act', lambda e: e.activation(pan_lq, pan_lq, AF.Ln, bias=cst['fc'].ap()[0:16, 1:2]))
        a_ln = op('act', lambda e: e.activation(pan_lk, pan_lk, AF.Ln, bias=cst['fc'].ap()[0:16, 1:2]), inc='sA')
        wait('dve', 'sA', a_ln)
        wait('dve', 'sV', v_gamma)
        op('dve', lambda e: e.tensor_scalar(comb1[64:80, :], pan_lq, LN128, -0.5,
                                            ALU.add, ALU.mult))
        op('dve', lambda e: e.scalar_tensor_tensor(comb1[0:16, :], pan_lk, -0.5, pan_lnb,
                                                   ALU.mult, ALU.add))
        op('dve', lambda e: e.tensor_tensor(comb1[0:16, :], comb1[0:16, :], pan_gam, ALU.add))
        v_comb = op('dve', lambda e: e.scalar_tensor_tensor(comb1[32:48, :], pan_lk, -0.5,
                                                            pan_gam, ALU.mult, ALU.subtract),
                    inc='sV')
        wait('act', 'sV', v_comb)
        a_bt = op('act', lambda e: e.activation(pan_lnb, pan_lnb, AF.Exp), inc='sA')
        wait('pe', 'sV', v_comb)
        a_cp = 0
        for ch in range(NCH):
            wait('pe', 'sA', a_cp)
            pc = op('pe', lambda e, ch=ch: e.transpose(
                P[6][:, 0:128], comb1[:, ch * 128:(ch + 1) * 128],
                cst['identf'].ap()), inc='sPE')
            wait('act', 'sPE', pc)
            a_cp = op('act', lambda e, ch=ch: e.copy(colp1[:, ch * 128:(ch + 1) * 128],
                                                     P[6][:, 0:128]), inc='sA')
        wait('pe', 'sA', a_bt)
        a_bp = a_cp
        for ch in range(NCH):
            wait('pe', 'sA', a_bp)
            pc = op('pe', lambda e, ch=ch: e.transpose(
                P[6][0:128, 128:144], pan_lnb[:, ch * 128:(ch + 1) * 128],
                cst['identf'].ap()[0:16, 0:16]), inc='sPE')
            wait('act', 'sPE', pc)
            a_bp = op('act', lambda e, ch=ch: e.copy(p_bet[:, ch * 16:(ch + 1) * 16],
                                                     P[6][0:128, 128:144]), inc='sA')
        a_gc = a_bp
        for ch in range(NCH):
            wait('pe', 'sA', a_gc)
            pc = op('pe', lambda e, ch=ch: e.transpose(
                P[6][0:1, 144:160], pan_gam[:, ch * 128 + 127:ch * 128 + 128],
                cst['identf'].ap()[0:16, 0:16]), inc='sPE')
            wait('act', 'sPE', pc)
            ag = op('act', lambda e, ch=ch: e.copy(gcrow[0:1, ch * 16:(ch + 1) * 16],
                                                   P[6][0:1, 144:160]), inc='sA')
            wait('pe', 'sA', ag)
            pc = op('pe', lambda e, ch=ch: e.matmul(
                P[6][:, 160:176], cst['onesf'].ap(), gcrow[0:1, ch * 16:(ch + 1) * 16],
                start=True, stop=True), inc='sPE')
            wait('act', 'sPE', pc)
            a_gc = op('act', lambda e, ch=ch: e.copy(gcbc[:, ch * 16:(ch + 1) * 16],
                                                     P[6][:, 160:176]), inc='sA')
        wait('act', 'sA', a_cp)
        for ch in range(NCH):
            op('act', lambda e, ch=ch: e.activation(
                p_phi[:, ch * 16:(ch + 1) * 16], colp1[:, ch * 128:ch * 128 + 16], AF.Exp))
            op('act', lambda e, ch=ch: e.activation(
                p_lam[:, ch * 16:(ch + 1) * 16], gcbc[:, ch * 16:(ch + 1) * 16], AF.Exp))
            op('act', lambda e, ch=ch: e.activation(
                self.p_rho[:, ch * 16:(ch + 1) * 16], colp1[:, ch * 128 + 64:ch * 128 + 80],
                AF.Exp), inc='sA' if ch == NCH - 1 else None)
        a_pan = c('sA')
        wait('dve', 'sA', a_pan)
        for ch in range(NCH):
            op('dve', lambda e, ch=ch: e.tensor_tensor(
                etmp[:, ch * 16:(ch + 1) * 16], colp1[:, ch * 128 + 32:ch * 128 + 48],
                gcbc[:, ch * 16:(ch + 1) * 16], ALU.add))
        v_et = op('dve', lambda e: e.engine_nop(), inc='sV')
        wait('act', 'sV', v_et)
        for ch in range(NCH):
            op('act', lambda e, ch=ch: e.activation(
                p_eta[:, ch * 16:(ch + 1) * 16], etmp[:, ch * 16:(ch + 1) * 16], AF.Exp),
               inc='sA' if ch == NCH - 1 else None)
        a_pan2 = c('sA')
        if 'panels' in self.debug:
            self.dbg('sync', colp1[:, 0:256], 256, [('sA', a_pan2)])
            self.dbg('sync', p_phi[:, 0:32], 32)
            self.dbg('sync', p_lam[:, 0:32], 32)
            self.dbg('sync', self.p_rho[:, 0:32], 32)
            self.dbg('sync', p_eta[:, 0:32], 32)
            self.dbg('sync', gcbc[:, 0:32], 32)

        if self.stop('combos'):
            return
        self.pbank = {}
        hB_pe = [0] * H
        headB_spill = []
        spill_prev = 0
        n_hB = 1 if 'donly' in self.debug else H
        for h in range(n_hB):
            qs = (h % 2) * 3 * QP
            wait('sync', 'sX', x_passA if h < 2 else headA_spill[h])
            if h >= 2:
                wait('sync', 'sPE', hB_pe[h - 2])
            op('sync', lambda e, h=h, qs=qs: e.dma_start(
                qkvA[:, qs:qs + 3 * QP].rearrange("p (m t) -> p m t", m=3),
                self.d_qkv[h].rearrange("m p t -> p m t")), inc='sX', incn=16)
            q_qkv = c('sX')
            if h >= 1:
                wait('sync', 'sPE', hB_pe[h - 1])
            else:
                wait('sync', 'sV', v_comb)
            for ri, src in enumerate([comb1[h:h + 1, :], comb1[32 + h:32 + h + 1, :],
                                      pan_gam[h:h + 1, :]]):
                op('sync', lambda e, ri=ri, src=src: e.dma_start(
                    rowst[0:1, ri * Q:(ri + 1) * Q], src), inc='sX', incn=16)
            q_rows = c('sX')
            wait('act', 'sX', spill_prev)
            wait('dve', 'sX', spill_prev)
            dargs = (qs, q_qkv, q_rows, qkvA, rowst, dwork, xapp,
                     blkst, colp1, p_phi, p_eta, p_lam, p_bet, gcbc, a_pan2, v_et)
            if 'donly' in self.debug:
                self.delta_chunk(h, 0, *dargs)
            else:
                for ch0 in range(0, NCH, 2):
                    self.delta_pair(h, ch0, *dargs)
            if 'donly' in self.debug:
                headB_spill.append(c('sX'))
                hB_pe[h] = c('sPE')
                continue
            self.compose(h, blkst, compw, mapst)
            wait('sync', 'sA', c('sA'))
            wait('sync', 'sV', c('sV'))
            spill_prev = op('sync', lambda e, h=h: e.dma_start(
                self.d_blk[h].rearrange("c p w -> p c w"),
                blkst.rearrange("p (c w) -> p c w", c=NCH)), inc='sX', incn=16)
            headB_spill.append(spill_prev)
            hB_pe[h] = c('sPE')

        if self.stop('passB'):
            return
        if 'donly' in self.debug and self.stop_stage == 'passB':
            self.barrier()
            return
        wait('sync', 'sA', c('sA'))
        mp = op('sync', lambda e: e.dma_start(self.d_maps, mapst), inc='sX', incn=16)
        wait('pool', 'sX', mp)
        cc = op('pool', lambda e: e.collective_compute(
            "AllGather", ALU.bypass, replica_groups=[[0, 1, 2, 3], [4, 5, 6, 7]],
            ins=[self.d_maps], outs=[self.d_gath]), inc='sC')

        self.emit_phase2(cc, headB_spill)

    def psumw(self, bank, tag=0):
        for sem, val in self.pbank.get(bank, ()):
            self.wait('pe', sem, val)

    def psums(self, bank, tag, *pairs):
        self.pbank[bank] = pairs

    def delta_chunk(self, h, ch, qs, q_qkv, q_rows, qkvA, rowst, dwork, xapp,
                    blkst, colp1, p_phi, p_eta, p_lam, p_bet, gcbc, a_pan2, v_et):
        st = self.delta_stages(h, ch, qs, q_qkv, q_rows, qkvA, rowst, dwork, xapp,
                               blkst, colp1, p_phi, p_eta, p_lam, p_bet, gcbc,
                               a_pan2, v_et)
        for fn in st:
            fn()

    def delta_pair(self, h, ch0, *args):
        stA = self.delta_stages(h, ch0, *args)
        stB = self.delta_stages(h, ch0 + 1, *args)
        for fa, fb in zip(stA, stB):
            fa()
            fb()

    def delta_stages(self, h, ch, qs, q_qkv, q_rows, qkvA, rowst, dwork, xapp,
                     blkst, colp1, p_phi, p_eta, p_lam, p_bet, gcbc, a_pan2, v_et):
        op, wait, c = self.op, self.wait, self.c
        cst, P = self.cst, self.P
        par = ch % 2

        qf = qkvA[:, qs + 0 * QP + 4 + ch * 128: qs + 0 * QP + 4 + (ch + 1) * 128]
        kf = qkvA[:, qs + 1 * QP + 4 + ch * 128: qs + 1 * QP + 4 + (ch + 1) * 128]
        vf = qkvA[:, qs + 2 * QP + 4 + ch * 128: qs + 2 * QP + 4 + (ch + 1) * 128]
        rowA = rowst[0:1, 0 * Q + ch * 128:0 * Q + (ch + 1) * 128]
        rowB = rowst[0:1, 1 * Q + ch * 128:1 * Q + (ch + 1) * 128]
        rowG = rowst[0:1, 2 * Q + ch * 128:2 * Q + (ch + 1) * 128]
        cA_col = colp1[:, ch * 128 + h:ch * 128 + h + 1]
        cB_col = colp1[:, ch * 128 + 32 + h:ch * 128 + 32 + h + 1]
        beta_col = p_bet[:, ch * 16 + h:ch * 16 + h + 1]
        phi_col = p_phi[:, ch * 16 + h:ch * 16 + h + 1]
        eta_col = p_eta[:, ch * 16 + h:ch * 16 + h + 1]
        lam_col = p_lam[:, ch * 16 + h:ch * 16 + h + 1]

        ds = par * 24 * 128
        S = lambda i: dwork[:, ds + i * 128:ds + (i + 1) * 128]
        bs = ch * 512
        blk = lambda i: blkst[:, bs + i * 128:bs + (i + 1) * 128]
        st = {}
        GB = 0 if par == 0 else 1

        def s1():
            wait('pe', 'sX', max(q_qkv, q_rows))
            self.psumw(0)
            op('pe', lambda e: e.matmul(P[0][:, 0:128], kf, kf, start=True, stop=True))
            op('pe', lambda e: e.matmul(P[0][:, 128:256], kf, qf, start=True, stop=True),
               inc='sPE')
            self.psumw(1)
            op('pe', lambda e: e.matmul(P[1][:, 0:128], cst['onesf'].ap(), rowB,
                                        start=True, stop=True))
            op('pe', lambda e: e.matmul(P[1][:, 128:256], cst['onesf'].ap(), rowA,
                                        start=True, stop=True))
            g2 = op('pe', lambda e: e.matmul(P[1][:, 256:384], cst['onesf'].ap(), rowG,
                                             start=True, stop=True), inc='sPE')
            wait('act', 'sPE', g2)
            if h == 0 and ch == 0:
                wait('act', 'sA', a_pan2)
                wait('act', 'sV', v_et)
            op('act', lambda e: e.activation(S(18), P[1][:, 0:128], AF.Exp, bias=cA_col))
            op('act', lambda e: e.activation(S(19), P[1][:, 128:256], AF.Exp, bias=cB_col))
            op('act', lambda e: e.activation(S(20), P[1][:, 256:384], AF.Exp, bias=cB_col))
            op('act', lambda e: e.activation(S(16), P[1][:, 256:384], AF.Exp))
            op('act', lambda e: e.copy(S(0), P[0][:, 0:128]))
            aG = op('act', lambda e: e.copy(S(1), P[0][:, 128:256]), inc='sA')
            self.psums(0, 0, ('sA', aG))
            self.psums(1, 0, ('sA', aG))
            st['aG'] = aG

        def s2():
            wait('dve', 'sA', st['aG'])
            op('dve', lambda e: e.tensor_tensor(S(21), S(18), S(0), ALU.mult))
            op('dve', lambda e: e.tensor_tensor(S(22), S(19), S(0), ALU.mult))
            op('dve', lambda e: e.tensor_tensor(S(23), S(20), S(1), ALU.mult))
            op('dve', lambda e: e.select(S(2), cst['msl'].ap(), S(21), cst['zero'].ap()))
            op('dve', lambda e: e.select(S(3), cst['msu'].ap(), S(22), cst['zero'].ap()))
            st['vM'] = op('dve', lambda e: e.select(S(4), cst['miu'].ap(), S(23),
                                                    cst['zero'].ap()), inc='sV')
            st['prevT'], st['prev'] = S(3), S(2)
            st['first'] = True

        def dbl(lvls):
            def f():
                if st.pop('first', False):
                    wait('pe', 'sV', st['vM'])
                prevT, prev = st['prevT'], st['prev']
                for lvl in lvls:
                    bk = 3 + (lvl % 2)
                    self.psumw(bk)
                    if lvl < 6:
                        op('pe', lambda e, bk=bk, pT=prevT, p=prev: e.matmul(
                            P[bk][:, 0:128], pT, p, start=True, stop=True))
                        pl = op('pe', lambda e, bk=bk, pT=prevT, p=prev: e.matmul(
                            P[bk][:, 128:256], p, pT, start=True, stop=True), inc='sPE')
                        wait('act', 'sPE', pl)
                        op('act', lambda e, bk=bk, lvl=lvl: e.copy(
                            dwork[:, ds + (10 + lvl) * 128:ds + (11 + lvl) * 128],
                            P[bk][:, 0:128]))
                        aL = op('act', lambda e, bk=bk, lvl=lvl: e.copy(
                            dwork[:, ds + (4 + lvl) * 128:ds + (5 + lvl) * 128],
                            P[bk][:, 128:256]), inc='sA')
                        prev = dwork[:, ds + (10 + lvl) * 128:ds + (11 + lvl) * 128]
                    else:
                        pl = op('pe', lambda e, bk=bk, pT=prevT, p=prev: e.matmul(
                            P[bk][:, 128:256], p, pT, start=True, stop=True), inc='sPE')
                        wait('act', 'sPE', pl)
                        aL = op('act', lambda e, bk=bk, lvl=lvl: e.copy(
                            dwork[:, ds + (4 + lvl) * 128:ds + (5 + lvl) * 128],
                            P[bk][:, 128:256]), inc='sA')
                    prevT = dwork[:, ds + (4 + lvl) * 128:ds + (5 + lvl) * 128]
                    self.psums(bk, 0, ('sA', aL))
                    wait('pe', 'sA', aL)
                st['prevT'], st['prev'] = prevT, prev
            return f

        def s5():
            b = self.pt_next
            self.pt_next = 1 - self.pt_next
            PT = self.PTb[b]
            xs = par * 512
            wait('pe', 'sA', self.pt_free[b])
            op('pe', lambda e, PT=PT: e.transpose(PT[:, 0:128], vf, cst['ident'].ap()))
            tk = op('pe', lambda e, PT=PT: e.transpose(PT[:, 128:256], kf, cst['ident'].ap()),
                    inc='sPE')
            wait('act', 'sPE', tk)
            op('act', lambda e, PT=PT: e.activation(xapp[:, xs:xs + 128], PT[:, 0:128],
                                                    AF.Copy, scale=beta_col))
            op('act', lambda e, PT=PT: e.activation(xapp[:, xs + 128:xs + 256],
                                                    PT[:, 128:256], AF.Copy, scale=phi_col))
            aX = op('act', lambda e, PT=PT: e.activation(S(17), PT[:, 128:256],
                                                         AF.Copy, scale=eta_col), inc='sA')
            self.pt_free[b] = aX
            st['aX'] = aX
            st['xcur'] = xs
            st['kapp'] = 0

        def app(ks):
            def f():
                xs = par * 512
                xcur = st['xcur']
                if st['kapp'] == 0:
                    wait('pe', 'sA', st['aX'])
                for k in ks:
                    bk = 5 + (k % 2)
                    self.psumw(bk)
                    lhs = S(3) if k == 0 else dwork[:, ds + (4 + k) * 128:ds + (5 + k) * 128]
                    pl = op('pe', lambda e, lhs=lhs, bk=bk, xcur=xcur: e.matmul(
                        P[bk][:, 0:256], lhs, xapp[:, xcur:xcur + 256], start=True, stop=True),
                        inc='sPE')
                    wait('dve', 'sPE', pl)
                    xnew = xs + 256 - (xcur - xs)
                    vX = op('dve', lambda e, bk=bk, xcur=xcur, xnew=xnew, k=k: e.tensor_tensor(
                        xapp[:, xnew:xnew + 256], xapp[:, xcur:xcur + 256], P[bk][:, 0:256],
                        ALU.subtract if k == 0 else ALU.add), inc='sV')
                    self.psums(bk, 0, ('sV', vX))
                    xcur = xnew
                    wait('pe', 'sV', vX)
                st['xcur'] = xcur
                st['kapp'] += len(ks)
            return f

        def s8():
            xcur = st['xcur']
            Wv = xapp[:, xcur:xcur + 128]
            Wk = xapp[:, xcur + 128:xcur + 256]
            self.psumw(6)
            op('pe', lambda e, Wv=Wv: e.matmul(P[6][:, 0:128], S(4), Wv, start=True, stop=True))
            op('pe', lambda e, Wk=Wk: e.matmul(P[6][:, 128:256], Wk, S(4), start=True, stop=True))
            op('pe', lambda e, Wv=Wv: e.matmul(P[6][:, 256:384], S(17), Wv, start=True, stop=True))
            g6 = op('pe', lambda e, Wk=Wk: e.matmul(P[6][:, 384:512], Wk, S(17),
                                                    start=True, stop=True), inc='sPE')
            wait('act', 'sPE', g6)
            op('act', lambda e: e.copy(blk(0), P[6][:, 0:128]))
            a_b = op('act', lambda e: e.copy(blk(3), P[6][:, 256:384]), inc='sA')
            wait('dve', 'sPE', g6)
            wait('dve', 'sA', a_b)
            op('dve', lambda e: e.tensor_tensor(S(21), qf, S(16), ALU.mult))
            op('dve', lambda e: e.scalar_tensor_tensor(
                blk(2), cst['ident'].ap(), lam_col, P[6][:, 384:512], ALU.mult, ALU.subtract))
            v6 = op('dve', lambda e: e.tensor_tensor(blk(1), S(21), P[6][:, 128:256],
                                                     ALU.subtract), inc='sV')
            self.psums(6, 0, ('sV', v6), ('sA', a_b))
            if ('blk00' in self.debug) and h == 0 and ch <= 1:
                self.dbg('sync', blkst[:, bs:bs + 512].bitcast(F32), 256,
                         [('sV', v6), ('sA', a_b)])

        return [s1, s2, dbl([1, 2]), dbl([3, 4]), dbl([5, 6]), s5,
                app([0, 1]), app([2, 3]), app([4, 5]), app([6]), s8]

    def compose(self, h, blkst, compw, mapst):
        op, wait, c = self.op, self.wait, self.c
        P = self.P
        CA = lambda p: compw[:, p * 128:(p + 1) * 128]
        CB = lambda p: compw[:, (2 + p) * 128:(3 + p) * 128]
        vv = c('sV')
        aa = c('sA')
        a0 = self.tr(blkst[:, 0 * 512 + 256:0 * 512 + 384], CA(0), pe_wait=('sV', vv))
        wait('dve', 'sV', vv)
        wait('dve', 'sA', aa)
        v0 = op('dve', lambda e: e.tensor_copy(CB(0), blkst[:, 384:512]), inc='sV')
        cap = 0
        for j in range(1, NCH):
            at = blkst[:, j * 512 + 256:j * 512 + 384]
            bj = blkst[:, j * 512 + 384:j * 512 + 512]
            self.psumw(6, 0)
            wait('pe', 'sA', a0)
            wait('pe', 'sV', v0)
            op('pe', lambda e, at=at, cap=cap: e.matmul(
                P[6][:, 0:128], at, CA(cap), start=True, stop=True))
            p2 = op('pe', lambda e, at=at, cap=cap: e.matmul(
                P[6][:, 128:256], at, CB(cap), start=True, stop=True), inc='sPE')
            wait('act', 'sPE', p2)
            a0 = op('act', lambda e, cap=cap: e.copy(CA(1 - cap), P[6][:, 0:128]), inc='sA')
            wait('dve', 'sPE', p2)
            wait('dve', 'sA', a0)
            v0 = op('dve', lambda e, bj=bj, cap=cap: e.tensor_tensor(
                CB(1 - cap), P[6][:, 128:256], bj, ALU.add), inc='sV')
            self.psums(6, 0, ('sV', v0), ('sA', a0))
            cap = 1 - cap
        self.tr(CA(cap), mapst[:, h * 256:h * 256 + 128], pe_wait=('sA', a0))
        wait('dve', 'sV', v0)
        op('dve', lambda e, cap=cap: e.tensor_copy(mapst[:, h * 256 + 128:h * 256 + 256],
                                                   CB(cap)), inc='sV')

    def emit_phase2(self, cc, headB_spill):
        op, wait, c = self.op, self.wait, self.c
        cst, stat, P = self.cst, self.stat, self.P

        self.areset(0)
        gath = self.buf(128, 4 * H * 256, BF16)
        psb = self.buf(128, 128, F32)
        psbf = self.buf(128, 128, BF16)
        self.sinit = self.buf(128, H * 128, F32)
        mark_2d = self.amark()
        wait('pool', 'sC', cc)
        gq = op('pool', lambda e: e.dma_start(
            gath.rearrange("p (r m) -> p r m", r=4),
            self.d_gath.rearrange("(r p) m -> p r m", p=128)), inc='sQ', incn=16)
        op('dve', lambda e: e.memset(self.sinit, 0.0))
        vz = op('dve', lambda e: e.memset(psb, 0.0), inc='sV')
        wait('dve', 'sQ', gq)
        wait('pe', 'sQ', gq)
        for h in range(H):
            vz = op('dve', lambda e: e.memset(psb, 0.0), inc='sV')
            vz = op('dve', lambda e: e.memset(psbf, 0.0), inc='sV')
            for r in range(3):
                at = gath[:, r * H * 256 + h * 256:r * H * 256 + h * 256 + 128]
                bt = gath[:, r * H * 256 + h * 256 + 128:r * H * 256 + h * 256 + 256]
                self.psumw(6, 0)
                wait('pe', 'sV', vz)
                pm = op('pe', lambda e, at=at: e.matmul(P[6][:, 0:128], at, psbf,
                                                        start=True, stop=True), inc='sPE')
                wait('dve', 'sPE', pm)
                vz = op('dve', lambda e, bt=bt: e.tensor_tensor(psb, P[6][:, 0:128], bt, ALU.add),
                        inc='sV')
                wait('dve', 'sV', vz)
                vz = op('dve', lambda e: e.tensor_copy(psbf, psb), inc='sV')
                self.psums(6, 0, ('sV', vz))
                vz = op('dve', lambda e, h=h, r=r: e.scalar_tensor_tensor(
                    self.sinit[:, h * 128:(h + 1) * 128], psb, cst['sel'].ap()[:, r + 1:r + 2],
                    self.sinit[:, h * 128:(h + 1) * 128], ALU.mult, ALU.add), inc='sV')
                wait('dve', 'sV', vz)
        v_sinit = c('sV')
        if 'sinit' in self.debug:
            self.dbg('sync', self.sinit[:, 0:256], 256, [('sV', v_sinit)])

        if self.stop('2i'):
            return
        self.areset(mark_2d)
        blk2 = self.buf(128, 2 * NCH * 512, BF16)
        zh2 = self.buf(128, 2 * Q, BF16)
        ost = self.buf(128, NCH * 128, BF16)
        sst = self.buf(128, 128, F32)
        sbf = self.buf(128, 128, BF16)
        ot1 = self.buf(128, 128, F32)
        ot2 = self.buf(128, 128, F32)
        ojk = self.buf(128, 128, BF16)
        yjk2 = self.buf(128, 128, BF16)
        o_spill = 0
        d2_pe = [0] * H
        for h in range(H):
            bs = (h % 2) * NCH * 512
            zs = (h % 2) * Q
            wait('sync', 'sX', headB_spill[h])
            if h >= 2:
                wait('sync', 'sPE', d2_pe[h - 2])
            op('sync', lambda e, h=h, bs=bs: e.dma_start(
                blk2[:, bs:bs + NCH * 512].rearrange("p (c w) -> p c w", c=NCH),
                self.d_blk[h].rearrange("c p w -> p c w")), inc='sX', incn=16)
            op('sync', lambda e, h=h, zs=zs: e.dma_start(zh2[:, zs:zs + Q], self.d_z[h]),
               inc='sX', incn=16)
            q_blk = c('sX')
            wait('dve', 'sV', v_sinit)
            op('dve', lambda e, h=h: e.tensor_copy(sst, self.sinit[:, h * 128:(h + 1) * 128]))
            v_s = op('dve', lambda e: e.tensor_copy(sbf, sst), inc='sV')
            wait('pe', 'sX', q_blk)
            for ch in range(NCH):
                b0 = bs + ch * 512
                rho_col = self.p_rho[:, ch * 16 + h:ch * 16 + h + 1]
                self.psumw(6, 0)
                wait('pe', 'sV', v_s)
                op('pe', lambda e, b0=b0: e.matmul(
                    P[6][:, 0:128], blk2[:, b0 + 128:b0 + 256], sbf, start=True, stop=True))
                pm = op('pe', lambda e, b0=b0: e.matmul(
                    P[6][:, 128:256], blk2[:, b0 + 256:b0 + 384], sbf, start=True, stop=True),
                    inc='sPE')
                wait('dve', 'sPE', pm)
                op('dve', lambda e, b0=b0: e.tensor_tensor(ot1, blk2[:, b0:b0 + 128],
                                                           P[6][:, 0:128], ALU.add))
                op('dve', lambda e, b0=b0: e.tensor_tensor(sst, P[6][:, 128:256],
                                                           blk2[:, b0 + 384:b0 + 512], ALU.add))
                op('dve', lambda e, rho_col=rho_col, zs=zs, ch=ch: e.scalar_tensor_tensor(
                    ot2, ot1, rho_col, zh2[:, zs + ch * 128:zs + (ch + 1) * 128],
                    ALU.mult, ALU.mult))
                v_s = op('dve', lambda e: e.tensor_copy(sbf, sst), inc='sV')
                self.psums(6, 0, ('sV', v_s))
                vr = op('dve', lambda e: e.scalar_tensor_tensor(
                    yjk2, ot2, 1.0, ot2, ALU.mult, ALU.mult, accum_out=stat[:, 8:9]), inc='sV')
                wait('act', 'sV', vr)
                op('act', lambda e: e.activation(stat[:, 9:10], stat[:, 8:9], AF.Ln,
                                                 bias=cst['fc'].ap()[:, 1:2], scale=1.0 / 128.0))
                ar = op('act', lambda e: e.activation(stat[:, 9:10], stat[:, 9:10], AF.Exp,
                                                      scale=-0.5), inc='sA')
                wait('dve', 'sA', ar)
                vn = op('dve', lambda e: e.scalar_tensor_tensor(
                    ojk, ot2, stat[:, 9:10], cst['nwbc'].ap(), ALU.mult, ALU.mult), inc='sV')
                self.tr(ojk, ost[:, ch * 128:(ch + 1) * 128], pe_wait=('sV', vn),
                        dst_wait=('sX', o_spill))
            a_h = c('sA')
            wait('sync', 'sA', a_h)
            o_spill = op('sync', lambda e, h=h: e.dma_start(self.d_onrm[h], ost),
                         inc='sX', incn=16)
            d2_pe[h] = c('sPE')
        x_onrm = c('sX')

        if self.stop('2d'):
            return
        self.barrier()
        self.areset(0)
        xn3 = self.buf(128, 16 * Q, BF16)
        self.dsum = self.buf(128, NCH * 4, F32)
        self.dssq = self.buf(128, NCH * 4, F32)
        mark_2b = self.amark()
        w2 = self.buf(128, 16 * D, BF16)
        oc2 = self.buf(128, 2 * 16 * 128, BF16)
        xc2 = self.buf(128, 2 * D, F32)
        attn2 = self.buf(128, D, F32)
        x2t = self.buf(128, D, F32)
        xjk = self.buf(128, D, BF16)
        wq = op('pool', lambda e: e.dma_start(
            w2.rearrange("p (v m) -> p v m", v=16),
            self.wout.rearrange("(v p) m -> p v m", p=128)), inc='sQ', incn=16)
        a23 = [0, 0]
        x2_dma = 0
        for chk in range(NCH):
            ob = (chk % 2) * 16 * 128
            xb = (chk % 2) * D
            wait('sync', 'sX', x_onrm)
            wait('sync', 'sV', a23[chk % 2])
            op('sync', lambda e, chk=chk, ob=ob: e.dma_start(
                oc2[:, ob:ob + 16 * 128].rearrange("p (v t) -> p v t", v=16),
                self.d_onrm[:, :, chk * 128:(chk + 1) * 128].rearrange("v p t -> p v t")),
                inc='sX', incn=16)
            op('sync', lambda e, chk=chk, xb=xb: e.dma_start(
                xc2[:, xb:xb + D], self.xt[4 + chk * 128:4 + (chk + 1) * 128, :]),
                inc='sX', incn=16)
            q_oc = c('sX')
            wait('pe', 'sX', q_oc)
            wait('pe', 'sQ', wq)
            for mN in range(4):
                bk2a = [0, 1, 3, 4][mN]
                self.psumw(bk2a, 2)
                pm = None
                for v in range(16):
                    pm = op('pe', lambda e, mN=mN, v=v, ob=ob, bk2a=bk2a: e.matmul(
                        P[bk2a][:, 0:512], oc2[:, ob + v * 128:ob + (v + 1) * 128],
                        w2[:, v * D + mN * 512:v * D + (mN + 1) * 512],
                        start=(v == 0), stop=(v == 15)), inc='sPE' if v == 15 else None)
                wait('dve', 'sPE', pm)
                ve = op('dve', lambda e, mN=mN, bk2a=bk2a: e.tensor_scalar(
                    attn2[:, mN * 512:(mN + 1) * 512], P[bk2a][:, 0:512], 1.0, 0.0,
                    ALU.mult, ALU.add, accum_out=stat[:, 16 + mN:17 + mN]), inc='sV')
                wait('dve', 'sV', ve)
                vv = op('dve', lambda e, mN=mN: e.scalar_tensor_tensor(
                    xjk[:, 0:512], attn2[:, mN * 512:(mN + 1) * 512], 1.0,
                    attn2[:, mN * 512:(mN + 1) * 512], ALU.mult, ALU.mult,
                    accum_out=stat[:, 20 + mN:21 + mN]), inc='sV')
                self.psums(bk2a, 2, ('sV', vv))
            v1 = op('dve', lambda e: e.reduce_sum(stat[:, 24:25], stat[:, 16:20],
                                                  axis=mybir.AxisListType.X), inc='sV')
            v2r = op('dve', lambda e: e.reduce_sum(stat[:, 25:26], stat[:, 20:24],
                                                   axis=mybir.AxisListType.X), inc='sV')
            wait('dve', 'sV', v2r)
            op('dve', lambda e: e.tensor_scalar(stat[:, 26:27], stat[:, 24:25], 1.0 / D,
                                                None, ALU.mult))
            v3r = op('dve', lambda e: e.tensor_scalar(stat[:, 28:29], stat[:, 25:26], 1.0 / D,
                                                      None, ALU.mult), inc='sV')
            wait('dve', 'sV', v3r)
            v4r = op('dve', lambda e: e.tensor_tensor(stat[:, 27:28], stat[:, 26:27],
                                                      stat[:, 26:27], ALU.mult), inc='sV')
            wait('dve', 'sV', v4r)
            v1 = op('dve', lambda e: e.tensor_tensor(stat[:, 28:29], stat[:, 28:29],
                                                     stat[:, 27:28], ALU.subtract), inc='sV')
            wait('act', 'sV', v1)
            op('act', lambda e: e.activation(stat[:, 29:30], stat[:, 28:29], AF.Ln, bias=cst['fc'].ap()[:, 0:1]))
            a1 = op('act', lambda e: e.activation(stat[:, 29:30], stat[:, 29:30], AF.Exp,
                                                  scale=-0.5), inc='sA')
            wait('dve', 'sA', a1)
            wait('dve', 'sX', x2_dma)
            v5r = op('dve', lambda e: e.tensor_scalar(stat[:, 30:31], stat[:, 26:27],
                                                      stat[:, 29:30], None, ALU.mult), inc='sV')
            wait('dve', 'sV', v5r)
            v6r = op('dve', lambda e: e.tensor_scalar(attn2, attn2, stat[:, 29:30], stat[:, 30:31],
                                                      ALU.mult, ALU.subtract), inc='sV')
            wait('dve', 'sV', v6r)
            vx2 = op('dve', lambda e, xb=xb: e.tensor_tensor(x2t, attn2, xc2[:, xb:xb + D],
                                                             ALU.add), inc='sV')
            wait('sync', 'sV', vx2)
            x2_dma = op('sync', lambda e, chk=chk: e.dma_start(self.d_x2[chk], x2t),
                        inc='sX', incn=16)
            vaa = op('dve', lambda e: e.tensor_scalar(x2t, x2t, 1.0, 0.0, ALU.mult, ALU.add,
                                                      accum_out=stat[:, 32:33]), inc='sV')
            vbb = op('dve', lambda e: e.scalar_tensor_tensor(xjk, x2t, 1.0, x2t, ALU.mult,
                                                             ALU.mult, accum_out=stat[:, 33:34]),
                     inc='sV')
            wait('dve', 'sV', vbb)
            op('dve', lambda e: e.tensor_scalar(stat[:, 34:35], stat[:, 32:33], 1.0 / D,
                                                None, ALU.mult))
            vcc = op('dve', lambda e: e.tensor_scalar(stat[:, 36:37], stat[:, 33:34], 1.0 / D,
                                                      None, ALU.mult), inc='sV')
            wait('dve', 'sV', vcc)
            vdd = op('dve', lambda e: e.tensor_tensor(stat[:, 35:36], stat[:, 34:35],
                                                      stat[:, 34:35], ALU.mult), inc='sV')
            wait('dve', 'sV', vdd)
            v2 = op('dve', lambda e: e.tensor_tensor(stat[:, 36:37], stat[:, 36:37],
                                                     stat[:, 35:36], ALU.subtract), inc='sV')
            wait('act', 'sV', v2)
            op('act', lambda e: e.activation(stat[:, 37:38], stat[:, 36:37], AF.Ln, bias=cst['fc'].ap()[:, 0:1]))
            a2 = op('act', lambda e: e.activation(stat[:, 37:38], stat[:, 37:38], AF.Exp,
                                                  scale=-0.5), inc='sA')
            wait('dve', 'sA', a2)
            vmr = op('dve', lambda e: e.tensor_scalar(stat[:, 38:39], stat[:, 34:35],
                                                      stat[:, 37:38], None, ALU.mult), inc='sV')
            wait('dve', 'sV', vmr)
            vxn = op('dve', lambda e: e.tensor_scalar(xjk, x2t, stat[:, 37:38], stat[:, 38:39],
                                                      ALU.mult, ALU.subtract), inc='sV')
            for g in range(2):
                self.tr_batch([(xjk[:, ft * 128:(ft + 1) * 128],
                                xn3[:, ft * Q + chk * 128:ft * Q + (chk + 1) * 128])
                               for ft in range(g * 8, g * 8 + 8)],
                              pe_wait=('sV', vxn) if g == 0 else None)
            a23[chk % 2] = op('dve', lambda e: e.engine_nop(), inc='sV')
        a_xn3 = c('sA')
        if 'x2' in self.debug:
            self.dbg('sync', attn2[:, 0:512], 512, [('sA', a_xn3)])
            self.dbg('sync', x2t[:, 0:512], 512)

        if self.stop('2a'):
            return
        self.barrier()
        self.areset(mark_2b)
        wgu = self.buf(128, 2 * 2 * 16 * 512, BF16)
        gu = self.buf(128, 2 * 512, BF16)
        fstage = self.buf(128, D, BF16)
        fb_done = [0, 0]
        gu_free = 0
        fs_dma = 0
        for mb in range(16):
            wb = (mb % 2) * 2 * 16 * 512
            wait('pool', 'sPE', fb_done[mb % 2])
            for wi, wsrc in enumerate([self.wg8, self.wu8]):
                op('pool', lambda e, wsrc=wsrc, mb=mb, wb=wb, wi=wi: e.dma_start(
                    wgu[:, wb + wi * 16 * 512:wb + (wi + 1) * 16 * 512].rearrange(
                        "p (kt m) -> p kt m", kt=16),
                    wsrc[:, mb * 512:(mb + 1) * 512].rearrange("(kt p) m -> p kt m", p=128)),
                    inc='sQ', incn=16)
            q_w = c('sQ')
            wait('pe', 'sQ', q_w)
            if mb == 0:
                wait('pe', 'sA', a_xn3)
            for mt in range(4):
                for nt in range(4):
                    it = mt * 4 + nt
                    bg, bu = [(0, 1), (3, 4), (5, 6)][it % 3]
                    self.psumw(bg, 3)
                    self.psumw(bu, 3)
                    for kt in range(16):
                        op('pe', lambda e, kt=kt, mt=mt, nt=nt, wb=wb, bg=bg: e.matmul(
                            P[bg][:, 0:512],
                            wgu[:, wb + kt * 512 + mt * 128:wb + kt * 512 + (mt + 1) * 128],
                            xn3[:, kt * Q + nt * 512:kt * Q + (nt + 1) * 512],
                            start=(kt == 0), stop=(kt == 15)))
                    pm = None
                    for kt in range(16):
                        pm = op('pe', lambda e, kt=kt, mt=mt, nt=nt, wb=wb, bu=bu: e.matmul(
                            P[bu][:, 0:512],
                            wgu[:, wb + 16 * 512 + kt * 512 + mt * 128:
                                wb + 16 * 512 + kt * 512 + (mt + 1) * 128],
                            xn3[:, kt * Q + nt * 512:kt * Q + (nt + 1) * 512],
                            start=(kt == 0), stop=(kt == 15)), inc='sPE' if kt == 15 else None)
                    wait('act', 'sPE', pm)
                    wait('act', 'sV', gu_free)
                    sg_col = cst['sgt'].ap()[:, mb * 4 + mt:mb * 4 + mt + 1]
                    su_col = cst['sut'].ap()[:, mb * 4 + mt:mb * 4 + mt + 1]
                    op('act', lambda e, sg_col=sg_col, bg=bg: e.activation(
                        gu[:, 0:512], P[bg][:, 0:512], AF.Silu, scale=sg_col))
                    aa = op('act', lambda e, su_col=su_col, bu=bu: e.activation(
                        gu[:, 512:1024], P[bu][:, 0:512], AF.Copy, scale=su_col), inc='sA')
                    self.psums(bg, 3, ('sA', aa))
                    self.psums(bu, 3, ('sA', aa))
                    wait('dve', 'sA', aa)
                    wait('dve', 'sX', fs_dma)
                    vv = op('dve', lambda e, nt=nt: e.tensor_tensor(
                        fstage[:, nt * 512:(nt + 1) * 512], gu[:, 0:512], gu[:, 512:1024],
                        ALU.mult), inc='sV')
                    gu_free = vv
                wait('sync', 'sV', vv)
                fs_dma = op('sync', lambda e, mb=mb, mt=mt: e.dma_start(
                    self.d_fuse[mb * 4 + mt], fstage), inc='sX', incn=16)
            fb_done[mb % 2] = c('sPE')
        x_fuse = c('sX')

        if self.stop('2b'):
            return
        self.barrier()
        self.areset(mark_2b)
        wdb = self.buf(128, 64 * 512, BF16)
        fus2 = self.buf(128, 2 * 64 * 128, BF16)
        ystage = self.buf(128, 2 * 512, BF16)
        yjk = self.buf(128, 512, BF16)
        sdbc = self.buf(128, D, F32)
        sdrow = self.buf(1, D, F32)
        op('sync', lambda e: e.dma_start(sdrow, self.sdr), inc='sX', incn=16)
        q_sd = c('sX')
        wait('pe', 'sX', q_sd)
        v_sd = 0
        for j in range(4):
            wait('pe', 'sV', v_sd)
            pj = op('pe', lambda e, j=j: e.matmul(P[6][:, 0:512], cst['onesf'].ap(),
                                                  sdrow[0:1, j * 512:(j + 1) * 512],
                                                  start=True, stop=True), inc='sPE')
            wait('dve', 'sPE', pj)
            v_sd = op('dve', lambda e, j=j: e.tensor_copy(sdbc[:, j * 512:(j + 1) * 512],
                                                          P[6][:, 0:512]), inc='sV')
        fu_free = [0, 0]
        ys_free = [0, 0]
        wd_pe = 0
        for mN in range(4):
            wait('pool', 'sX', x_fuse)
            wait('pool', 'sPE', wd_pe)
            op('pool', lambda e, mN=mN: e.dma_start(
                wdb.rearrange("p (it m) -> p it m", it=64),
                self.wd8[:, mN * 512:(mN + 1) * 512].rearrange("(it p) m -> p it m", p=128)),
                inc='sQ', incn=16)
            q_wd = c('sQ')
            for chk in range(NCH):
                fb = (chk % 2) * 64 * 128
                wait('sync', 'sX', x_fuse)
                wait('sync', 'sPE', fu_free[chk % 2])
                op('sync', lambda e, chk=chk, fb=fb: e.dma_start(
                    fus2[:, fb:fb + 64 * 128].rearrange("p (it t) -> p it t", it=64),
                    self.d_fuse[:, :, chk * 128:(chk + 1) * 128].rearrange("it p t -> p it t")),
                    inc='sX', incn=16)
                q_f = c('sX')
                wait('pe', 'sX', q_f)
                wait('pe', 'sQ', q_wd)
                bk = chk % 2
                self.psumw(bk, 4)
                pm = None
                for it in range(64):
                    pm = op('pe', lambda e, it=it, fb=fb, bk=bk: e.matmul(
                        P[bk][:, 0:512], fus2[:, fb + it * 128:fb + (it + 1) * 128],
                        wdb[:, it * 512:(it + 1) * 512],
                        start=(it == 0), stop=(it == 63)), inc='sPE' if it == 63 else None)
                fu_free[chk % 2] = pm
                wd_pe = pm
                wait('dve', 'sPE', pm)
                wait('dve', 'sV', v_sd)
                wait('dve', 'sX', ys_free[chk % 2])
                ys = (chk % 2) * 512
                vy = op('dve', lambda e, mN=mN, bk=bk, ys=ys, chk=chk: e.scalar_tensor_tensor(
                    ystage[:, ys:ys + 512], P[bk][:, 0:512], 1.0,
                    sdbc[:, mN * 512:(mN + 1) * 512], ALU.mult, ALU.mult,
                    accum_out=self.dsum[:, chk * 4 + mN:chk * 4 + mN + 1]), inc='sV')
                wait('dve', 'sV', vy)
                vv = op('dve', lambda e, ys=ys, chk=chk, mN=mN: e.scalar_tensor_tensor(
                    yjk, ystage[:, ys:ys + 512], 1.0, ystage[:, ys:ys + 512],
                    ALU.mult, ALU.mult,
                    accum_out=self.dssq[:, chk * 4 + mN:chk * 4 + mN + 1]), inc='sV')
                self.psums(bk, 4, ('sV', vv))
                wait('sync', 'sV', vv)
                ys_free[chk % 2] = op('sync', lambda e, mN=mN, chk=chk, ys=ys: e.dma_start(
                    self.d_x4p[mN, chk], ystage[:, ys:ys + 512]), inc='sX', incn=16)
        x_x4p = c('sX')
        v_2c = c('sV')

        if self.stop('2c'):
            return
        self.barrier()
        self.areset(mark_2b)
        yst = self.buf(128, 2 * D, BF16)
        xc4 = self.buf(128, 2 * D, F32)
        x4t = self.buf(128, D, F32)
        x4b = self.buf(128, D, F32)
        out_dma = 0
        dd_free = [0, 0]
        for chk in range(NCH):
            yb = (chk % 2) * D
            wait('sync', 'sX', x_x4p)
            wait('sync', 'sV', dd_free[chk % 2])
            op('sync', lambda e, chk=chk, yb=yb: e.dma_start(
                yst[:, yb:yb + D].rearrange("p (m t) -> p m t", m=4),
                self.d_x4p[:, chk].rearrange("m p t -> p m t")), inc='sX', incn=16)
            op('sync', lambda e, chk=chk, yb=yb: e.dma_start(
                xc4[:, yb:yb + D], self.d_x2[chk]), inc='sX', incn=16)
            q_y = c('sX')
            wait('dve', 'sX', q_y)
            wait('dve', 'sV', v_2c)
            v1r = op('dve', lambda e, chk=chk: e.reduce_sum(
                stat[:, 40:41], self.dsum[:, chk * 4:(chk + 1) * 4], axis=mybir.AxisListType.X),
                inc='sV')
            v2r = op('dve', lambda e, chk=chk: e.reduce_sum(
                stat[:, 41:42], self.dssq[:, chk * 4:(chk + 1) * 4], axis=mybir.AxisListType.X),
                inc='sV')
            wait('dve', 'sV', v2r)
            op('dve', lambda e: e.tensor_scalar(stat[:, 42:43], stat[:, 40:41], 1.0 / D,
                                                None, ALU.mult))
            v3r = op('dve', lambda e: e.tensor_scalar(stat[:, 44:45], stat[:, 41:42], 1.0 / D,
                                                      None, ALU.mult), inc='sV')
            wait('dve', 'sV', v3r)
            v4r = op('dve', lambda e: e.tensor_tensor(stat[:, 43:44], stat[:, 42:43],
                                                      stat[:, 42:43], ALU.mult), inc='sV')
            wait('dve', 'sV', v4r)
            v1 = op('dve', lambda e: e.tensor_tensor(stat[:, 44:45], stat[:, 44:45],
                                                     stat[:, 43:44], ALU.subtract), inc='sV')
            wait('act', 'sV', v1)
            op('act', lambda e: e.activation(stat[:, 45:46], stat[:, 44:45], AF.Ln, bias=cst['fc'].ap()[:, 0:1]))
            a1 = op('act', lambda e: e.activation(stat[:, 45:46], stat[:, 45:46], AF.Exp,
                                                  scale=-0.5), inc='sA')
            wait('dve', 'sA', a1)
            wait('dve', 'sX', out_dma)
            vmr = op('dve', lambda e: e.tensor_scalar(stat[:, 46:47], stat[:, 42:43],
                                                      stat[:, 45:46], None, ALU.mult), inc='sV')
            wait('dve', 'sV', vmr)
            vap = op('dve', lambda e, yb=yb: e.tensor_scalar(x4t, yst[:, yb:yb + D],
                                                             stat[:, 45:46], stat[:, 46:47],
                                                             ALU.mult, ALU.subtract), inc='sV')
            wait('dve', 'sV', vap)
            vo = op('dve', lambda e, yb=yb: e.tensor_tensor(x4b, x4t, xc4[:, yb:yb + D],
                                                            ALU.add), inc='sV')
            dd_free[chk % 2] = vo
            wait('sync', 'sV', vo)
            out_dma = op('sync', lambda e, chk=chk: e.dma_start(
                self.y[chk * 128:(chk + 1) * 128, :], x4b), inc='sX', incn=16)
        x_out = c('sX')
        wait('sync', 'sX', x_out)
        wait('pool', 'sX', x_out)


def _prep_inputs(inputs):
    x = np.asarray(inputs['L0_attn_hidden_states'], np.float32)
    wqkv = np.ascontiguousarray(np.asarray(inputs['L0_attn_W_qkv'], np.float32))
    conv_w = np.asarray(inputs['L0_attn_conv_w'], np.float32)[:, 0, :]
    wz = np.ascontiguousarray(np.asarray(inputs['L0_attn_W_z'], np.float32))
    wb = np.asarray(inputs['L0_attn_W_b'], np.float32)
    wa = np.asarray(inputs['L0_attn_W_a'], np.float32)
    dtb = np.asarray(inputs['L0_attn_dt_bias'], np.float32)
    alog = np.asarray(inputs['L0_attn_A_log'], np.float32)
    nw = np.asarray(inputs['L0_attn_norm_weight'], np.float32)
    wout = np.ascontiguousarray(np.asarray(inputs['L0_attn_W_out'], np.float32))
    wg8 = np.ascontiguousarray(np.asarray(inputs['L0_ffn_w_gate'], np.int8))
    wu8 = np.ascontiguousarray(np.asarray(inputs['L0_ffn_w_up'], np.int8))
    wd8 = np.ascontiguousarray(np.asarray(inputs['L0_ffn_w_down'], np.int8))
    sg = np.asarray(inputs['L0_ffn_s_gate'], np.float32)
    su = np.asarray(inputs['L0_ffn_s_up'], np.float32)
    sd = np.asarray(inputs['L0_ffn_s_down'], np.float32)

    wba = np.zeros((D, 64), np.float32)
    wba[:, 0:16] = wb
    wba[:, 32:48] = wa
    dtab = np.zeros((64, 2), np.float32)
    dtab[32:48, 0] = dtb
    dtab[32:48, 1] = alog
    cw = np.zeros((H * 3 * 128, 4), np.float32)
    for h in range(H):
        for m in range(3):
            cw[h * 384 + m * 128:h * 384 + (m + 1) * 128, :] = \
                conv_w[m * D + h * 128:m * D + (h + 1) * 128, :]
    sgt = np.ascontiguousarray(sg.reshape(64, 128).T)
    sut = np.ascontiguousarray(su.reshape(64, 128).T)
    sdr = np.ascontiguousarray(sd.reshape(1, D))
    nwr = np.ascontiguousarray(nw.reshape(1, 128))

    idx = np.arange(128)
    m_sl = (idx[:, None] > idx[None, :]).astype(np.int8)
    m_su = (idx[:, None] < idx[None, :]).astype(np.int8)
    m_iu = (idx[:, None] <= idx[None, :]).astype(np.int8)

    common = dict(wqkv=wqkv, convw=cw, wz=wz, wba=wba, dtab=dtab, nw=nwr, wout=wout,
                  wg8=wg8, wu8=wu8, wd8=wd8, sgt=sgt, sut=sut, sdr=sdr,
                  ident=np.eye(128, dtype=BF), identf=np.eye(128, dtype=np.float32),
                  onesb=np.ones((1, 128), BF), onesf=np.ones((1, 128), np.float32),
                  onescol=np.ones((128, 1), BF), ones16=np.ones((16, 128), np.float32),
                  m_sl=m_sl, m_su=m_su, m_iu=m_iu,
                  fconst=np.broadcast_to(np.array([1e-5, 1e-6, 1.0, 0.0], np.float32), (128, 4)).copy())
    in_maps = []
    for core in range(8):
        b, q = core // 4, core % 4
        xm = np.zeros((QP, D), np.float32)
        if q == 0:
            xm[4:, :] = x[b, 0:Q, :]
        else:
            xm[:, :] = x[b, q * Q - 4:(q + 1) * Q, :]
        selm = np.zeros((128, 4), np.float32)
        selm[:, q] = 1.0
        m = dict(common)
        m['xt'] = xm
        m['sel'] = selm
        in_maps.append(m)
    return in_maps


def kernel(**inputs):
    in_maps = _prep_inputs(inputs)
    nc = Builder().build()
    res = run_bass_kernel_spmd(nc, in_maps, core_ids=list(range(8)))
    out = np.zeros((B, T, D), np.float32)
    for core in range(8):
        b, q = core // 4, core % 4
        out[b, q * Q:(q + 1) * Q, :] = res.results[core]['y']
    return out


if __name__ == '__main__':
    import time
    t0 = time.time()
    nc = Builder().build()
    n = sum(1 for _ in nc.all_instructions())
    print(f'build ok: {n} instructions in {time.time()-t0:.1f}s')
